# version 01 (final)
# baseline (speedup 1.0000x reference)
import os
import numpy as np
import concourse.bass as bass
import concourse.mybir as mybir
from concourse.bass_utils import run_bass_kernel_spmd

F32 = mybir.dt.float32
BF16 = mybir.dt.bfloat16
I32 = mybir.dt.int32
ALU = mybir.AluOpType
AF = mybir.ActivationFunctionType
AX = mybir.AxisListType

B, S, D, DCH, NTT = 2, 2048, 1024, 8, 16
NH, NG = 16, 4
NE, EI = 15, 512

_A2PI = 2 * np.pi
A_HI = float(np.float32(np.float64(round(_A2PI * 2**11)) / 2**11))
A_MID = float(np.float32(np.float64(round((_A2PI - A_HI) * 2**34)) / 2**34))
A_LO = float(np.float32(_A2PI - A_HI - A_MID))
INV_2PI = float(np.float32(1.0 / _A2PI))
HALF_PI = float(np.float32(np.pi / 2))
_LN1E4 = float(np.log(10000.0))
L_HI = float(np.float32(np.float64(round(_LN1E4 * 2**12)) / 2**12))
L_LO = float(np.float32(_LN1E4 - L_HI))

PHASES = int(os.environ.get("ATHENA_PHASES", "15"))
DUMP = int(os.environ.get("ATHENA_DUMP", "0"))
NB = int(os.environ.get("ATHENA_NB", "2"))
NGE = int(os.environ.get("ATHENA_NG", "4"))
DEPTH = int(os.environ.get("ATHENA_DEPTH", "9"))
QSLIM = int(os.environ.get("ATHENA_QS", "99"))
KUNITS = int(os.environ.get("ATHENA_KUNITS", "99"))
NOBIAS = int(os.environ.get("ATHENA_NOBIAS", "0"))


class Sem:
    __slots__ = ("h", "name", "n")

    def __init__(self, handle, name):
        self.h, self.name, self.n = handle, name, 0


class Prog:
    def __init__(self):
        self.items = []

    def emit(self, eng, fn, waits=(), incs=()):
        waits = tuple((s, int(v)) for (s, v) in waits if v > 0)
        for (s, k) in incs:
            s.n += k
        self.items.append((eng, waits, fn, tuple(incs)))

    def lower(self, block):
        def mk(engname):
            def body(eng):
                for (e, waits, fn, incs) in self.items:
                    if e != engname:
                        continue
                    for (s, v) in waits:
                        eng.wait_ge(s.h, v)
                    ins = fn(eng)
                    for (s, k) in incs:
                        ins = ins.then_inc(s.h, k)
            return body
        block.tensor(mk("tensor"))
        block.vector(mk("vector"))
        block.scalar(mk("scalar"))
        block.gpsimd(mk("gpsimd"))
        block.sync(mk("sync"))


def _consts_np():
    import ml_dtypes
    bf = ml_dtypes.bfloat16
    ident = np.eye(128, dtype=np.float32).astype(bf)
    sw = np.zeros((128, 128), dtype=np.float32)
    for p in range(128):
        sw[p ^ 32, p] = 1.0
    swap64 = sw.astype(bf)
    tri = (np.arange(128)[:, None] <= np.arange(128)[None, :]).astype(np.float32).astype(bf)
    halfrow = np.zeros((1, 256), dtype=np.float32)
    halfrow[0, 0:64] = 1.0
    halfrow[0, 128 + 64:256] = 1.0
    p = np.arange(128)
    cols = np.stack([(p & 31).astype(np.float32),
                     np.where((p & 32) == 0, -1.0, 1.0).astype(np.float32),
                     p.astype(np.float32)], axis=1).astype(np.float32)
    e0m = np.zeros((32, 128), dtype=np.float32)
    e0m[0, :] = 1.0
    halfsel = np.zeros((32, 256), dtype=np.float32)
    halfsel[0, 0:64] = 1.0
    halfsel[0, 128 + 64:256] = 1.0
    return ident, swap64, tri, halfrow, cols, e0m.astype(bf), halfsel


def build():
    nc = bass.Bass("TRN2", target_bir_lowering=False,
                   dynamic_dma_scratch_size=7680)

    def din(name, shape, dtype=F32):
        return nc.dram_tensor(name, list(shape), dtype, kind="ExternalInput").ap()

    x_in = din("x", [B, S, D])
    kvp_in = din("kvp", [NG, 128, 4096])
    qop_in = din("qop", [NG, 128, 4096])
    qkvb_in = din("qkv_b", [128, 24])
    outb_in = din("out_b", [D])
    wqT_in = din("wqT", [8, 128, 1024])
    wkT_in = din("wkT", [8, 128, 1024])
    wvT_in = din("wvT", [4, 128, 2048])
    rw_in = din("router_w", [128, 128])
    rb_in = din("router_b", [16])
    ew_in = din("ew", [NE, 128, 12288])
    cv_in = din("convT", [128, 12288])
    shs_in = din("shs", [128, 8192])

    y_out = nc.dram_tensor("y", [B, S, D], F32, kind="ExternalOutput").ap()
    ewb = nc.dram_tensor("ewb", [NE, 128, 12288], BF16, kind="Internal").ap()
    cvb = nc.dram_tensor("cvb", [128, 12288], BF16, kind="Internal").ap()
    shsb = nc.dram_tensor("shsb", [128, 8192], BF16, kind="Internal").ap()
    kvpb = nc.dram_tensor("kvpb", [NG, 128, 4096], BF16, kind="Internal").ap()
    qopb = nc.dram_tensor("qopb", [NG, 128, 4096], BF16, kind="Internal").ap()
    wqb = nc.dram_tensor("wqb", [8, 128, 1024], BF16, kind="Internal").ap()
    wkb = nc.dram_tensor("wkb", [8, 128, 1024], BF16, kind="Internal").ap()
    wvb = nc.dram_tensor("wvb", [4, 128, 2048], BF16, kind="Internal").ap()
    dumps = {}
    if DUMP:
        for nm, cols in (("A1", 16384), ("A2", 16384), ("A3", 9984), ("bcsb", 1536)):
            dumps[nm] = nc.dram_tensor("d_" + nm, [128, cols], BF16,
                                       kind="ExternalOutput").ap()
        for nm, cols in (("small", 48), ("stats", 32)):
            dumps[nm] = nc.dram_tensor("d_" + nm, [128, cols], F32,
                                       kind="ExternalOutput").ap()

    identN, swapN, triN, halfN, colsN, e0N, hselN = _consts_np()
    identD = nc.inline_tensor(identN, name="identD").ap()
    swapD = nc.inline_tensor(swapN, name="swapD").ap()
    triD = nc.inline_tensor(triN, name="triD").ap()
    halfD = nc.inline_tensor(halfN, name="halfD").ap()
    colsD = nc.inline_tensor(colsN, name="colsD").ap()
    e0D = nc.inline_tensor(e0N, name="e0D").ap()
    hselD = nc.inline_tensor(hselN, name="hselD").ap()

    import contextlib
    ctx = contextlib.ExitStack()

    def sbuf(name, shape, dtype):
        return ctx.enter_context(nc.sbuf_tensor(name, list(shape), dtype))

    with ctx:
        resid = sbuf("resid", [128, B * NTT * D], F32)
        ident = sbuf("ident", [128, 128], BF16)
        swapm = sbuf("swapm", [128, 128], BF16)
        tri = sbuf("tri", [128, 128], BF16)
        e0m = sbuf("e0m", [32, 128], BF16)
        ccols = sbuf("ccols", [128, 3], F32)
        onescol = sbuf("onescol", [128, 1], BF16)
        stats = sbuf("stats", [128, 2 * NTT], F32)
        small = sbuf("small", [128, 48], F32)
        qkvbc = sbuf("qkvbc", [128, 24], F32)
        rb16 = sbuf("rb16", [32, 16], BF16)
        ivf = sbuf("ivf", [128, 8], F32)
        comb = sbuf("comb", [128, 8 * 16], BF16)
        bcsb = sbuf("bcsb", [128, 1536], BF16)
        A1 = sbuf("A1", [128, 16384], BF16)
        A2 = sbuf("A2", [128, 16384], BF16)
        A3 = sbuf("A3", [128, 9984], BF16)

        PA = ctx.enter_context(nc.psum_tensor("PA", [128, 2048], F32))
        PB = ctx.enter_context(nc.psum_tensor("PB", [128, 2048], F32))
        PT = [PB[:, 0:512].bitcast(BF16), PB[:, 512:1024].bitcast(BF16)]

        def S_(name):
            return Sem(ctx.enter_context(nc.semaphore(name)), name)

        s_ld, s_xd, s_mm = S_("ld"), S_("xd"), S_("mm")
        s_ev, s_ac, s_gp = S_("ev"), S_("ac"), S_("gp")
        s_w0, s_w1, s_w2 = S_("w0"), S_("w1"), S_("w2")
        s_stg = S_("stg")
        s_sm = S_("sm")

        prog = Prog()
        globals_gp = [0]

        def gdma(dst, src, waits=(), slow=False, sem=None):
            sem = sem if sem is not None else s_ld
            waits = tuple(waits) + ((s_gp, globals_gp[0]),)
            if slow:
                prog.emit("gpsimd", lambda g, d=dst, s=src: g.dma_start(
                    out=d, in_=s, allow_slow_non_contiguous=True),
                    waits=waits, incs=((sem, 16),))
            else:
                prog.emit("gpsimd", lambda g, d=dst, s=src: g.dma_start(out=d, in_=s),
                          waits=waits, incs=((sem, 16),))
            return sem.n

        def wld(dst, src, waits=(), slow=False, sem=None):
            sem = sem if sem is not None else s_xd
            prog.emit("sync", lambda g, d=dst, s=src: g.dma_start(out=d, in_=s),
                      waits=waits, incs=((sem, 16),))
            return sem.n

        def xdma(dst, src, waits=()):
            prog.emit("sync", lambda g, d=dst, s=src: g.dma_start(out=d, in_=s),
                      waits=waits, incs=((s_xd, 16),))
            return s_xd.n

        def ev(fn, waits=()):
            prog.emit("vector", fn, waits=waits, incs=((s_ev, 1),))
            return s_ev.n

        def ea(fn, waits=()):
            prog.emit("scalar", fn, waits=waits, incs=((s_ac, 1),))
            return s_ac.n

        def egp(fn, waits=()):
            prog.emit("gpsimd", fn, waits=waits, incs=((s_gp, 1),))
            return s_gp.n

        def et(fn, waits=()):
            prog.emit("tensor", fn, waits=waits, incs=((s_mm, 1),))
            return s_mm.n

        gdma(ident[:], identD)
        gdma(swapm[:], swapD)
        gdma(tri[:], triD)
        gdma(e0m[:], e0D)
        gdma(ccols[:], colsD)
        egp(lambda g: g.memset(onescol[:], 1.0))
        egp(lambda g: g.memset(small[:, 30:31], 1e-5))
        egp(lambda g: g.memset(A1[:, :], 0.0))
        egp(lambda g: g.memset(A2[:, :], 0.0))
        egp(lambda g: g.memset(A3[:, :], 0.0))
        egp(lambda g: g.memset(small[:, 0:30], 0.0))
        egp(lambda g: g.memset(small[:, 31:48], 0.0))
        egp(lambda g: g.memset(bcsb[:, :], 0.0))
        egp(lambda g: g.memset(stats[:, :], 0.0))
        globals_gp[0] = s_gp.n
        egp(lambda g: g.memset(rb16[:], 0.0))
        gdma(rb16[0:1, :], rb_in[:].rearrange("(o d) -> o d", o=1),
             waits=((s_gp, s_gp.n),))
        gdma(qkvbc[:, 0:24], qkvb_in)
        consts_ld = s_ld.n

        for g_ in range(NG):
            gdma(kvpb[g_], kvp_in[g_], sem=s_stg)
        for g_ in range(NG):
            gdma(qopb[g_], qop_in[g_], sem=s_stg)
        stage_attn_a = s_stg.n
        prog.emit("gpsimd", lambda g: g.nop(), waits=((s_stg, stage_attn_a),))
        for c_ in range(8):
            gdma(wkb[c_], wkT_in[c_], sem=s_stg)
        for c_ in range(8):
            gdma(wqb[c_], wqT_in[c_], sem=s_stg)
        for d_ in range(4):
            gdma(wvb[d_], wvT_in[d_], sem=s_stg)
        stage_attn_b = s_stg.n
        prog.emit("gpsimd", lambda g: g.nop(), waits=((s_stg, stage_attn_b),))
        gdma(cvb, cv_in, sem=s_stg)
        gdma(shsb, shs_in, sem=s_stg)
        stage_cv = s_stg.n
        prog.emit("gpsimd", lambda g: g.nop(), waits=((s_stg, stage_cv),))
        for e_ in range(NE):
            gdma(ewb[e_], ew_in[e_], sem=s_stg)
        stage_ew = s_stg.n

        def mk_ivf(dstcol, jap, shi, slo, base):
            c0 = ev(lambda v, jap=jap, base=base: v.tensor_scalar(
                small[:, 0:1], jap, float(base), None, ALU.add),
                waits=((s_ld, consts_ld), (s_ev, s_ev.n), (s_ac, s_ac.n),
                       (s_gp, s_gp.n)))
            c1 = ev(lambda v, slo=slo: v.tensor_scalar(
                small[:, 1:2], small[:, 0:1], float(-slo), None, ALU.mult),
                waits=((s_ev, c0),))
            c2 = ev(lambda v, shi=shi: v.scalar_tensor_tensor(
                small[:, 2:3], small[:, 0:1], float(-shi), small[:, 1:2],
                ALU.mult, ALU.add), waits=((s_ev, c1),))
            return ea(lambda a, d=dstcol: a.activation(d, small[:, 2:3], AF.Exp),
                      waits=((s_ev, c2), (s_ac, s_ac.n)))

        iv_ready = mk_ivf(ivf[:, 0:1], ccols[:, 0:1], L_HI / 32, L_LO / 32, 0)
        for c in range(4):
            iv_ready = mk_ivf(ivf[:, 1 + c:2 + c], ccols[:, 2:3],
                              L_HI / 512, L_LO / 512, 128 * c)

        xrd = 0
        for b in range(B):
            xrd = xdma(resid[:, b * NTT * D:(b + 1) * NTT * D]
                       .rearrange("p (tt d) -> p tt d", tt=NTT),
                       x_in[b].rearrange("(tt p) d -> p tt d", p=128))

        resid_ev = [0] * (B * NTT)
        resid_mm = [0] * (B * NTT)

        def gen_table(dst, ivcol, t0, tlen, cos, tgA, tgB, tgI, signcol=None,
                      xw=()):
            ti = egp(lambda g, t0=t0, L=tlen, tgA=tgA: g.iota(
                tgA[:, 0:L], [[1, L]], base=t0, channel_multiplier=0,
                allow_small_or_imprecise_dtypes=True),
                waits=((s_gp, s_gp.n), (s_ev, s_ev.n), (s_ac, s_ac.n)) + tuple(xw))
            c0 = ev(lambda v, iv=ivcol, L=tlen, cos=cos, tgA=tgA, tgB=tgB:
                    v.tensor_scalar(tgB[:, 0:L], tgA[:, 0:L], iv,
                                    HALF_PI if cos else 0.0, ALU.mult, ALU.add),
                    waits=((s_gp, ti), (s_ac, iv_ready), (s_ev, s_ev.n)))
            c1 = ev(lambda v, L=tlen, tgA=tgA, tgB=tgB: v.tensor_scalar(
                tgA[:, 0:L], tgB[:, 0:L], INV_2PI, None, ALU.mult),
                waits=((s_ev, c0),))
            c2 = ev(lambda v, L=tlen, tgA=tgA, tgI=tgI: v.tensor_copy(
                tgI[:, 0:L], tgA[:, 0:L]), waits=((s_ev, c1),))
            c3 = ev(lambda v, L=tlen, tgA=tgA, tgI=tgI: v.tensor_copy(
                tgA[:, 0:L], tgI[:, 0:L]), waits=((s_ev, c2),))
            cc = c3
            for coef in (A_HI, A_MID, A_LO):
                cc = ev(lambda v, L=tlen, coef=coef, tgA=tgA, tgB=tgB:
                        v.scalar_tensor_tensor(tgB[:, 0:L], tgA[:, 0:L], -coef,
                                               tgB[:, 0:L], ALU.mult, ALU.add),
                        waits=((s_ev, cc),))
            PI = float(np.float32(np.pi))
            for (cmp_op, cmp_v, sgn) in ((ALU.is_gt, PI, -1.0), (ALU.is_lt, -PI, 1.0)):
                cc = ev(lambda v, L=tlen, tgA=tgA, tgB=tgB, cmp_op=cmp_op,
                        cmp_v=cmp_v: v.tensor_scalar(
                            tgA[:, 0:L], tgB[:, 0:L], cmp_v, None, cmp_op),
                        waits=((s_ev, cc),))
                for coef in (A_HI, A_MID, A_LO):
                    cc = ev(lambda v, L=tlen, coef=coef, sgn=sgn, tgA=tgA, tgB=tgB:
                            v.scalar_tensor_tensor(tgB[:, 0:L], tgA[:, 0:L],
                                                   sgn * coef, tgB[:, 0:L],
                                                   ALU.mult, ALU.add),
                            waits=((s_ev, cc),))
            if signcol is None:
                aa = ea(lambda a, d=dst, L=tlen, tgB=tgB: a.activation(
                    d, tgB[:, 0:L], AF.Sin), waits=((s_ev, cc), (s_ac, s_ac.n)))
                return aa, cc
            aa = ea(lambda a, L=tlen, tgA=tgA, tgB=tgB: a.activation(
                tgA[:, 0:L], tgB[:, 0:L], AF.Sin),
                waits=((s_ev, cc), (s_ac, s_ac.n)))
            c8 = ev(lambda v, d=dst, L=tlen, sg=signcol, tgA=tgA: v.tensor_scalar(
                d, tgA[:, 0:L], sg, None, ALU.mult), waits=((s_ac, aa),))
            return aa, c8

        ln_pt = [0, 0]
        ln_xt = [0, 0]

        def phase_reset():
            ln_pt[0] = ln_pt[1] = s_ev.n
            ln_xt[0] = ln_xt[1] = s_mm.n

        def ln_xhat(b, tt, sc, dstcol, xtok, i2, extra_w=()):
            rt = resid[:, (b * NTT + tt) * D:(b * NTT + tt + 1) * D]
            mcol = stats[:, 2 * sc:2 * sc + 1]
            rcol = stats[:, 2 * sc + 1:2 * sc + 2]
            v0 = ev(lambda v, rt=rt: v.bn_stats(small[:, 8:14], rt[:, 0:512]),
                    waits=((s_xd, xrd), (s_ev, resid_ev[b * NTT + tt]),
                           (s_ev, s_ev.n), (s_gp, globals_gp[0])) + tuple(extra_w))
            v0 = ev(lambda v, rt=rt: v.bn_stats(small[:, 14:20], rt[:, 512:1024]),
                    waits=((s_ev, v0),))
            v1 = ev(lambda v: v.bn_aggr(small[:, 20:22], small[:, 8:20]),
                    waits=((s_ev, v0),))
            v2 = ev(lambda v, m=mcol: v.tensor_copy(m, small[:, 20:21]),
                    waits=((s_ev, v1),))
            a0 = ea(lambda a: a.activation(small[:, 22:23], small[:, 21:22], AF.Ln,
                    bias=small[:, 30:31], scale=1.0),
                    waits=((s_ev, v1), (s_ac, s_ac.n)))
            a1 = ea(lambda a, rc=rcol: a.activation(rc, small[:, 22:23], AF.Exp,
                    bias=0.0, scale=-0.5), waits=((s_ac, a0),))
            v3 = ev(lambda v, m=mcol, rc=rcol: v.tensor_scalar(
                small[:, 23:24], m, rc, -1.0, ALU.mult, ALU.mult),
                waits=((s_ac, a1), (s_ev, v2)))
            a2 = ea(lambda a, xt=xtok, rt=rt, rc=rcol: a.activation(
                xt, rt, AF.Identity, bias=small[:, 23:24], scale=rc),
                waits=((s_ev, v3), (s_mm, ln_xt[i2]), (s_ac, s_ac.n)))

            def tfn(t, xt=xtok, i2=i2):
                ins = None
                for c in range(DCH):
                    ins = t.transpose(PT[i2][:, c * 128:(c + 1) * 128],
                                      xt[:, c * 128:(c + 1) * 128], ident[:])
                return ins
            tr = et(tfn, waits=((s_ac, a2), (s_ld, consts_ld), (s_ev, ln_pt[i2])))
            ln_xt[i2] = tr
            last = 0
            for c in range(DCH):
                d = dstcol(c)
                last = ev(lambda v, d=d, c=c, i2=i2: v.tensor_copy(
                    d, PT[i2][:, c * 128:(c + 1) * 128]), waits=((s_mm, tr),))
            ln_pt[i2] = last
            return last

        a3_qtmp = A3[:, 0:512]
        a3_rt1 = A3[:, 512:1024]
        a3_rt2 = A3[:, 1024:1536]
        a3_qs = A3[:, 1536:2048]
        a3_ctab = A3[:, 2048:4096]
        a3_stab = A3[:, 4096:6144]
        a3_tgA = A3[:, 6144:7168].bitcast(F32)
        a3_tgB = A3[:, 7168:8192].bitcast(F32)
        a3_tgI = A3[:, 8192:9216].bitcast(I32)
        a3_xtok = [A3[:, 6144:7168], A3[:, 7168:8192]]
        a2_K = A2[:, 0:4096]
        a2_V = A2[:, 4096:8192]
        a2_oT = A2[:, 8192:10240]
        a2_ws = [A2[:, 10240:12288], A2[:, 12288:14336]]
        a2_E = A2[:, 14336:16384]

        if PHASES & 1:
            tab_fin = []
            for tsl in range(4):
                aa1, ee1 = gen_table(a3_ctab[:, tsl * 512:(tsl + 1) * 512],
                                     ivf[:, 0:1], tsl * 512, 512, True,
                                     a3_tgA, a3_tgB, a3_tgI)
                aa2, ee2 = gen_table(a3_stab[:, tsl * 512:(tsl + 1) * 512],
                                     ivf[:, 0:1], tsl * 512, 512, False,
                                     a3_tgA, a3_tgB, a3_tgI, signcol=ccols[:, 1:2])
            tab_ac, tab_ev = s_ac.n, s_ev.n

            for b in range(NB):
                phase_reset()
                xe = 0
                for tt in range(NTT):
                    i2 = tt % 2
                    xe = ln_xhat(b, tt, tt,
                                 lambda c, tt=tt: A1[:, c * 2048 + tt * 128:
                                                     c * 2048 + (tt + 1) * 128],
                                 a3_xtok[i2], i2,
                                 extra_w=((s_ev, tab_ev), (s_ac, tab_ac)))
                xhat_ready = xe
                if DEPTH < 1:
                    continue

                proj_ev = s_ev.n
                for g in range(NGE):
                    lwk = wld(A2[:, 10240:14336], kvpb[g],
                              waits=((s_mm, s_mm.n), (s_stg, stage_attn_a)),
                              sem=s_w0)
                    lwv = lwk

                    kev = proj_ev
                    kunit = 0
                    for c2 in range(2):
                        for ts in range(4):
                            kunit += 1
                            if kunit > KUNITS:
                                continue
                            pk = PA[:, (ts % 2) * 512:(ts % 2) * 512 + 512]

                            def kmm(t, c2=c2, ts=ts, pk=pk):
                                ins = None
                                for fc in range(DCH):
                                    ins = t.matmul(
                                        pk,
                                        a2_ws[0][:, fc * 256 + c2 * 128:fc * 256 + c2 * 128 + 128],
                                        A1[:, fc * 2048 + ts * 512:fc * 2048 + (ts + 1) * 512],
                                        start=(fc == 0), stop=(fc == 7))
                                return ins
                            mmk = et(kmm, waits=((s_w0, lwk), (s_ev, xhat_ready),
                                                 (s_ev, kev)))
                            ae = ea(lambda a, pk=pk, c2=c2, g=g: a.activation(
                                a3_qtmp, pk, AF.Identity,
                                bias=qkvbc[:, 8 + g * 2 + c2:9 + g * 2 + c2],
                                scale=1.0), waits=((s_mm, mmk), (s_ac, s_ac.n)))
                            mms = et(lambda t, ae=ae: t.matmul(
                                PA[:, 1024:1536], swapm[:], a3_qtmp[:, 0:512],
                                start=True, stop=True), waits=((s_ac, ae),))
                            r1 = ev(lambda v, ts=ts: v.tensor_tensor(
                                a3_rt1, a3_qtmp,
                                a3_ctab[:, ts * 512:(ts + 1) * 512], ALU.mult),
                                waits=((s_ac, ae), (s_ev, s_ev.n)))
                            r2 = ev(lambda v, ts=ts: v.tensor_tensor(
                                a3_rt2, PA[:, 1024:1536],
                                a3_stab[:, ts * 512:(ts + 1) * 512], ALU.mult),
                                waits=((s_mm, mms), (s_ev, r1)))
                            kev = ev(lambda v, c2=c2, ts=ts: v.tensor_tensor(
                                a2_K[:, c2 * 2048 + ts * 512:c2 * 2048 + (ts + 1) * 512],
                                a3_rt1, a3_rt2, ALU.add), waits=((s_ev, r2),))
                    k_ready = kev
                    if DEPTH < 2:
                        continue

                    vev = 0
                    for tt in range(NTT):
                        pv = PA[:, (tt % 2) * 512:(tt % 2) * 512 + 256]

                        def vmm(t, tt=tt, pv=pv, g=g):
                            ins = None
                            for fc in range(DCH):
                                ins = t.matmul(
                                    pv,
                                    A1[:, fc * 2048 + tt * 128:fc * 2048 + (tt + 1) * 128],
                                    a2_ws[1][:, fc * 256:(fc + 1) * 256],
                                    start=(fc == 0), stop=(fc == 7))
                            return ins
                        mmv = et(vmm, waits=((s_w0, lwv), (s_ev, xhat_ready),
                                             (s_ev, vev), (s_ev, k_ready)))
                        vev = ev(lambda v, tt=tt, pv=pv: v.tensor_copy(
                            a2_V[:, tt * 256:(tt + 1) * 256], pv),
                            waits=((s_mm, mmv),))
                    v_ready = vev
                    if DEPTH < 3:
                        continue

                    lwq = wld(A2[:, 10240:14336], qopb[g],
                              waits=((s_mm, s_mm.n),), sem=s_w0)
                    lwo = lwq

                    for hp in range(2):
                        evac_ev = max(v_ready, k_ready, proj_ev, s_ev.n)
                        for qs in range(4):
                            pq = PB[:, 1024:1536]

                            def qmm(t, hp=hp, qs=qs, pq=pq):
                                ins = None
                                for fc in range(DCH):
                                    ins = t.matmul(
                                        pq,
                                        a2_ws[0][:, fc * 256 + hp * 128:fc * 256 + hp * 128 + 128],
                                        A1[:, fc * 2048 + qs * 512:fc * 2048 + (qs + 1) * 512],
                                        start=(fc == 0), stop=(fc == 7))
                                return ins
                            mmq = et(qmm, waits=((s_w0, lwq), (s_ev, evac_ev)))
                            aq = ea(lambda a, pq=pq, hp=hp, g=g: a.activation(
                                a3_qtmp, pq, AF.Identity,
                                bias=qkvbc[:, g * 2 + hp:g * 2 + hp + 1],
                                scale=1.0), waits=((s_mm, mmq), (s_ac, s_ac.n)))
                            mmsw = et(lambda t, aq=aq: t.matmul(
                                PB[:, 1536:2048], swapm[:], a3_qtmp[:, 0:512],
                                start=True, stop=True), waits=((s_ac, aq),))
                            q1 = ev(lambda v, qs=qs: v.tensor_tensor(
                                a3_rt1, a3_qtmp,
                                a3_ctab[:, qs * 512:(qs + 1) * 512], ALU.mult),
                                waits=((s_ac, aq), (s_ev, s_ev.n)))
                            q2 = ev(lambda v, qs=qs: v.tensor_tensor(
                                a3_rt2, PB[:, 1536:2048],
                                a3_stab[:, qs * 512:(qs + 1) * 512], ALU.mult),
                                waits=((s_mm, mmsw), (s_ev, q1)))
                            qrdy = ev(lambda v: v.tensor_tensor(
                                a3_qs, a3_rt1, a3_rt2, ALU.add),
                                waits=((s_ev, q2),))
                            if DEPTH < 4:
                                evac_ev = qrdy
                                continue

                            shr = A3[0:1, 8192:9216]
                            red_ev = 0
                            tr_mm = 0
                            for qt in range(4):
                                def mpm(t, qt=qt, hp=hp):
                                    ins = None
                                    for kc4 in range(4):
                                        ins = t.matmul(
                                            PA[:, kc4 * 512:(kc4 + 1) * 512],
                                            a3_qs[0:64, qt * 128:(qt + 1) * 128],
                                            a2_K[0:64, hp * 2048 + kc4 * 512:hp * 2048 + (kc4 + 1) * 512],
                                            start=True, stop=True,
                                            tile_position=(0, 0),
                                            skip_group_check=True)
                                        ins = t.matmul(
                                            PB[:, kc4 * 512:(kc4 + 1) * 512],
                                            a3_qs[64:128, qt * 128:(qt + 1) * 128],
                                            a2_K[64:128, hp * 2048 + kc4 * 512:hp * 2048 + (kc4 + 1) * 512],
                                            start=True, stop=True,
                                            tile_position=(64, 0),
                                            skip_group_check=True)
                                    return ins
                                mpw = [(s_ev, qrdy), (s_ev, evac_ev)]
                                if red_ev:
                                    mpw.append((s_ev, red_ev))
                                mm_mx = et(mpm, waits=tuple(mpw))
                                r1_ = ev(lambda v, qt=qt: v.reduce_max(
                                    small[:, 40 + qt:41 + qt], PA[:, 0:2048],
                                    axis=AX.X), waits=((s_mm, mm_mx), (s_ev, s_ev.n)))
                                red_ev = ev(lambda v, qt=qt: v.reduce_max(
                                    small[:, 44 + qt:45 + qt], PB[:, 0:2048],
                                    axis=AX.X), waits=((s_ev, r1_),))
                            cb16 = ev(lambda v: v.tensor_copy(
                                a3_rt1[:, 0:8], small[:, 40:48]),
                                waits=((s_ev, red_ev),))

                            def trm(t):
                                ins = None
                                for i8 in range(8):
                                    ins = t.transpose(
                                        PT[0][0:1, i8 * 128:(i8 + 1) * 128],
                                        a3_rt1[:, i8:i8 + 1], ident[:])
                                return ins
                            tr_mm = et(trm, waits=((s_ev, cb16),))
                            sh_ev = 0
                            for i8 in range(8):
                                qt, h = i8 % 4, i8 // 4
                                sh_ev = ev(lambda v, i8=i8, qt=qt, h=h: v.tensor_scalar(
                                    shr[0:1, h * 512 + qt * 128:h * 512 + (qt + 1) * 128],
                                    PT[0][0:1, i8 * 128:(i8 + 1) * 128], -1.0, None,
                                    ALU.mult), waits=((s_mm, tr_mm), (s_ev, s_ev.n)))
                            shift_rdy = sh_ev
                            if DEPTH < 5:
                                evac_ev = sh_ev
                                continue

                            exp_cnt = [0, 0]
                            av_cnt = [0, 0]
                            for kc in range(16):
                                sp = PA[:, (kc % 2) * 1024:(kc % 2) * 1024 + 1024]

                                def smm(t, kc=kc, sp=sp, hp=hp):
                                    t.matmul(sp[:, 0:512],
                                             a2_K[0:64, hp * 2048 + kc * 128:hp * 2048 + (kc + 1) * 128],
                                             a3_qs[0:64, :], start=True, stop=False,
                                             tile_position=(0, 0),
                                             skip_group_check=True)
                                    t.matmul(
                                        sp[:, 512:1024],
                                        a2_K[64:128, hp * 2048 + kc * 128:hp * 2048 + (kc + 1) * 128],
                                        a3_qs[64:128, :], start=True, stop=False,
                                        tile_position=(64, 0), skip_group_check=True)
                                    t.matmul(sp[:, 0:512], e0m[:, :],
                                             A3[0:32, 8192:8704], start=False,
                                             stop=True, skip_group_check=True)
                                    return t.matmul(sp[:, 512:1024], e0m[:, :],
                                                    A3[0:32, 8704:9216], start=False,
                                                    stop=True, skip_group_check=True)
                                w = [(s_ev, qrdy), (s_ev, shift_rdy)]
                                if kc >= 2:
                                    w.append((s_ac, exp_cnt[kc % 2]))
                                mmS = et(smm, waits=tuple(w))
                                slot = kc % 2
                                wv_ = [(s_mm, mmS)]
                                if kc >= 2:
                                    wv_.append((s_mm, av_cnt[slot]))
                                exp_cnt[slot] = ea(
                                    lambda a, sp=sp, slot=slot: a.activation(
                                        a2_E[:, slot * 1024:(slot + 1) * 1024], sp,
                                        AF.Exp, scale=0.125),
                                    waits=tuple(wv_) + ((s_ac, s_ac.n),))

                                def avmm(t, kc=kc, slot=slot, hp=hp):
                                    ed = a2_E[:, slot * 1024:(slot + 1) * 1024]
                                    vt = a2_V[:, kc * 256 + hp * 128:kc * 256 + hp * 128 + 128]
                                    t.matmul(PB[0:64, 0:512], vt[:, 0:64],
                                             ed[:, 0:512], start=(kc == 0),
                                             stop=(kc == 15), tile_position=(0, 0),
                                             skip_group_check=True)
                                    t.matmul(PB[64:128, 512:1024], vt[:, 64:128],
                                             ed[:, 512:1024], start=(kc == 0),
                                             stop=(kc == 15), tile_position=(0, 64),
                                             skip_group_check=True)
                                    t.matmul(PB[0:1, 1024:1536], onescol[:],
                                             ed[:, 0:512], start=(kc == 0),
                                             stop=(kc == 15), tile_position=(0, 0),
                                             skip_group_check=True)
                                    return t.matmul(
                                        PB[32:33, 1024:1536], onescol[:],
                                        ed[:, 512:1024], start=(kc == 0),
                                        stop=(kc == 15), tile_position=(0, 32),
                                        skip_group_check=True)
                                wa_ = [(s_ac, exp_cnt[slot])]
                                if kc == 0:
                                    wa_.append((s_ev, evac_ev))
                                av_cnt[slot] = et(avmm, waits=tuple(wa_))
                            av_done = av_cnt[15 % 2]

                            rc1 = ev(lambda v: v.reciprocal(
                                bcsb[0:1, 512:1024], PB[0:1, 1024:1536]),
                                waits=((s_mm, av_done), (s_ev, s_ev.n)))
                            rc2 = ev(lambda v: v.reciprocal(
                                bcsb[0:1, 1024:1536], PB[32:33, 1024:1536]),
                                waits=((s_ev, rc1),))

                            def bcmm(t):
                                t.matmul(PA[:, 0:512], e0m[:, :],
                                         bcsb[0:32, 512:1024], start=True, stop=False,
                                         skip_group_check=True,
                                         tile_position=(0, 0))
                                return t.matmul(PA[64:128, 0:512], e0m[:, 0:64],
                                                bcsb[0:32, 1024:1536], start=True,
                                                stop=True, skip_group_check=True,
                                                tile_position=(0, 64))
                            mmb = et(bcmm, waits=((s_ev, rc2), (s_ac, exp_cnt[0])))
                            cb = ev(lambda v: v.tensor_copy(
                                bcsb[:, 0:512], PA[:, 0:512]),
                                waits=((s_mm, mmb),))
                            vb = qkvbc[:, 16 + g * 2 + hp:17 + g * 2 + hp]
                            e1 = ev(lambda v, qs=qs, vb=vb: v.tensor_tensor(
                                a2_oT[0:64, qs * 512:(qs + 1) * 512],
                                PB[0:64, 0:512], bcsb[0:64, 0:512], ALU.mult),
                                waits=((s_ev, cb),))
                            e1 = ev(lambda v, qs=qs, vb=vb: v.tensor_scalar(
                                a2_oT[0:64, qs * 512:(qs + 1) * 512],
                                a2_oT[0:64, qs * 512:(qs + 1) * 512], vb[0:64],
                                None, ALU.add), waits=((s_ev, e1),))
                            e2 = ev(lambda v, qs=qs: v.tensor_tensor(
                                a2_oT[64:128, qs * 512:(qs + 1) * 512],
                                PB[64:128, 512:1024], bcsb[64:128, 0:512],
                                ALU.mult), waits=((s_ev, e1),))
                            evac_ev = ev(lambda v, qs=qs, vb=vb: v.tensor_scalar(
                                a2_oT[64:128, qs * 512:(qs + 1) * 512],
                                a2_oT[64:128, qs * 512:(qs + 1) * 512], vb[64:128],
                                None, ALU.add), waits=((s_ev, e2),))
                        outT_ready = evac_ev

                        pr_ev = outT_ready
                        for tt in range(NTT if DEPTH >= 6 else 0):
                            pp = PA[:, (tt % 2) * 1024:(tt % 2) * 1024 + 1024]

                            def pmm(t, tt=tt, pp=pp, g=g, hp=hp):
                                ins = None
                                for fo in range(2):
                                    ins = t.matmul(
                                        pp[:, fo * 512:(fo + 1) * 512],
                                        a2_oT[:, tt * 128:(tt + 1) * 128],
                                        a2_ws[1][:, hp * 1024 + fo * 512:hp * 1024 + (fo + 1) * 512],
                                        start=True, stop=True,
                                        skip_group_check=True)
                                return ins
                            mmp = et(pmm, waits=((s_ev, pr_ev), (s_w0, lwo)))
                            rt = resid[:, (b * NTT + tt) * D:(b * NTT + tt + 1) * D]
                            pr_ev = ev(lambda v, rt=rt, pp=pp: v.tensor_tensor(
                                rt, rt, pp, ALU.add),
                                waits=((s_mm, mmp),
                                       (s_ev, resid_ev[b * NTT + tt])))
                            resid_ev[b * NTT + tt] = pr_ev
                    proj_ev = pr_ev

                ob_row = A3[0:1, 512:1536]
                lob = gdma(ob_row, outb_in[:].rearrange("(o d) -> o d", o=1),
                           waits=((s_ev, s_ev.n),), sem=s_sm)
                bias_mm_prev = s_ev.n
                for tt in range(NTT if not NOBIAS else 0):
                    pp = PA[:, (tt % 2) * 1024:(tt % 2) * 1024 + 1024]

                    def bmm(t, pp=pp):
                        ins = None
                        for fo in range(2):
                            ins = t.matmul(pp[:, fo * 512:(fo + 1) * 512],
                                           e0m[:, :],
                                           A3[0:32, 512 + fo * 512:512 + (fo + 1) * 512],
                                           start=True, stop=True,
                                           skip_group_check=True)
                        return ins
                    mmb2 = et(bmm, waits=((s_sm, lob), (s_ev, bias_mm_prev)))
                    rt = resid[:, (b * NTT + tt) * D:(b * NTT + tt + 1) * D]
                    bias_mm_prev = ev(lambda v, rt=rt, pp=pp: v.tensor_tensor(
                        rt, rt, pp, ALU.add),
                        waits=((s_mm, mmb2), (s_ev, resid_ev[b * NTT + tt])))
                    resid_ev[b * NTT + tt] = bias_mm_prev

        def ffn_phase(which):
            XSP = 1056
            h_base, sg_base, xtok_base = 8448, 12544, 14592
            slots = [
                {"g": A3[:, 0:4096], "u": A3[:, 4096:8192], "d": A2[:, 12288:16384]},
                {"g": A2[:, 0:4096], "u": A2[:, 4096:8192], "d": A2[:, 8192:12288]},
            ]
            slot_mm = [s_mm.n, s_mm.n]
            phase_w = ((s_ev, s_ev.n), (s_ac, s_ac.n))
            rw_sb = A3[:, 8192:8448]
            lrw = gdma(rw_sb[:, 0:128], rw_in,
                       waits=((s_mm, s_mm.n),) + phase_w, sem=s_sm)

            for ci in range(4):
                b, th = ci // 2, (ci % 2)
                tts = range(th * 8, th * 8 + 8)
                phase_reset()
                if th == 0:
                    pdone = 0
                    for c in range(DCH):
                        pdone = ev(lambda v, c=c: v.memset(
                            A1[:, c * XSP:c * XSP + 2], 0.0),
                            waits=((s_mm, s_mm.n), (s_ev, s_ev.n)))
                else:
                    pdone = 0
                    for c in range(DCH):
                        pdone = ev(lambda v, c=c: v.tensor_copy(
                            A1[:, c * XSP:c * XSP + 2],
                            A1[:, c * XSP + 1024:c * XSP + 1026]),
                            waits=((s_mm, s_mm.n), (s_ev, s_ev.n)))
                xe = 0
                for i, tt in enumerate(tts):
                    i2 = i % 2
                    xtok = A1[:, xtok_base + 512 * 0:xtok_base + 1024] if False else \
                        A1[:, xtok_base:xtok_base + 1024]
                    xe = ln_xhat(b, tt, i, lambda c, i=i: A1[
                        :, c * XSP + 2 + i * 128:c * XSP + 2 + (i + 1) * 128],
                        xtok, 0, extra_w=((s_ev, pdone),))
                xhat_ready = xe

                cmb_ev = 0
                for i, tt in enumerate(tts):
                    def rmm(t, i=i):
                        for fc in range(DCH):
                            t.matmul(PA[:, 0:16],
                                     A1[:, fc * XSP + 2 + i * 128:fc * XSP + 2 + (i + 1) * 128],
                                     rw_sb[:, fc * 16:(fc + 1) * 16],
                                     start=(fc == 0), stop=False,
                                     skip_group_check=True)
                        return t.matmul(PA[:, 0:16], e0m[:, :], rb16[0:32, :],
                                        start=False, stop=True,
                                        skip_group_check=True)
                    mmr = et(rmm, waits=((s_sm, lrw), (s_ev, xhat_ready),
                                         (s_ev, cmb_ev)))
                    u0 = ev(lambda v: v.reduce_max(small[:, 20:21], PA[:, 0:16],
                            axis=AX.X, negate=True),
                            waits=((s_mm, mmr), (s_ev, s_ev.n)))
                    u1 = ea(lambda a: a.activation(small[:, 24:40], PA[:, 0:16],
                            AF.Exp, bias=small[:, 20:21], scale=1.0),
                            waits=((s_ev, u0), (s_ac, s_ac.n)))
                    u2 = ev(lambda v: v.reduce_sum(small[:, 21:22],
                            small[:, 24:40], axis=AX.X), waits=((s_ac, u1),))
                    u3 = ev(lambda v: v.reciprocal(small[:, 22:23],
                            small[:, 21:22]), waits=((s_ev, u2),))
                    u4 = ev(lambda v: v.tensor_scalar(small[:, 24:40],
                            small[:, 24:40], small[:, 22:23], None, ALU.mult),
                            waits=((s_ev, u3),))
                    u5 = ev(lambda v: v.max(small[:, 40:48], small[:, 24:40]),
                            waits=((s_ev, u4),))
                    u6 = ev(lambda v: v.tensor_scalar(small[:, 0:16],
                            small[:, 24:40], small[:, 41:42], None, ALU.is_ge),
                            waits=((s_ev, u5),))
                    u7 = ev(lambda v: v.tensor_tensor(small[:, 0:16],
                            small[:, 0:16], small[:, 24:40], ALU.mult),
                            waits=((s_ev, u6),))
                    u8 = ev(lambda v: v.tensor_tensor(small[:, 20:21],
                            small[:, 40:41], small[:, 41:42], ALU.add),
                            waits=((s_ev, u7),))
                    u9 = ev(lambda v: v.tensor_scalar(small[:, 21:22],
                            small[:, 20:21], 1e-6, None, ALU.add),
                            waits=((s_ev, u8),))
                    u10 = ev(lambda v: v.reciprocal(small[:, 22:23],
                             small[:, 21:22]), waits=((s_ev, u9),))
                    u11 = ev(lambda v, i=i: v.tensor_scalar(
                        comb[:, i * 16:(i + 1) * 16], small[:, 0:16],
                        small[:, 22:23], None, ALU.mult), waits=((s_ev, u10),))
                    sc = i
                    mcol = stats[:, 2 * sc:2 * sc + 1]
                    rcol = stats[:, 2 * sc + 1:2 * sc + 2]
                    w0 = ev(lambda v, i=i, rc=rcol: v.tensor_scalar(
                        small[:, 20:21], comb[:, i * 16 + 15:i * 16 + 16], rc,
                        None, ALU.mult), waits=((s_ev, u11),))
                    w1 = ev(lambda v: v.tensor_scalar(small[:, 21:22],
                            small[:, 20:21], 1.0, 1.0, ALU.mult, ALU.add),
                            waits=((s_ev, w0),))
                    w2 = ev(lambda v, m=mcol: v.tensor_scalar(small[:, 22:23],
                            small[:, 20:21], m, -1.0, ALU.mult, ALU.mult),
                            waits=((s_ev, w1),))
                    rt = resid[:, (b * NTT + tt) * D:(b * NTT + tt + 1) * D]
                    w3 = ev(lambda v, rt=rt: v.tensor_scalar(
                        rt, rt, small[:, 21:22], small[:, 22:23], ALU.mult,
                        ALU.add), waits=((s_ev, w2),
                                         (s_ev, resid_ev[b * NTT + tt])))
                    resid_ev[b * NTT + tt] = w3
                    cmb_ev = w3
                comb_ready = cmb_ev

                sl_load = [0, 0]
                for e in range(NE):
                    sl = e % 2
                    wz = slots[sl]
                    wsem = s_w0 if sl == 0 else s_w1
                    if sl == 0:
                        lw = wld(A3[:, 0:8192], ewb[e][:, 0:8192],
                                 waits=((s_mm, slot_mm[sl]), (s_stg, stage_ew))
                                 + (phase_w if e < 2 else ()),
                                 sem=wsem)
                        lw = wld(A2[:, 12288:16384], ewb[e][:, 8192:12288],
                                 sem=wsem)
                    else:
                        lw = wld(A2[:, 0:12288], ewb[e],
                                 waits=((s_mm, slot_mm[sl]), (s_stg, stage_ew))
                                 + (phase_w if e < 2 else ()),
                                 sem=wsem)
                    sl_load[sl] = lw
                    for ts in range(2):
                        def gmm(t, ts=ts, wz=wz, i0=0):
                            ins = None
                            for ic in range(4):
                                for fc in range(DCH):
                                    ins = t.matmul(
                                        PA[:, ic * 512:(ic + 1) * 512],
                                        wz["g"][:, fc * 512 + ic * 128:fc * 512 + ic * 128 + 128],
                                        A1[:, fc * XSP + 2 + ts * 512:fc * XSP + 2 + (ts + 1) * 512],
                                        start=(fc == 0), stop=(fc == 7))
                            return ins
                        wgate = [(s_w0 if sl == 0 else s_w1, sl_load[sl]),
                                 (s_ev, comb_ready),
                                 (s_ac, s_ac.n), (s_ev, s_ev.n)]
                        mmg = et(gmm, waits=tuple(wgate))

                        def umm(t, ts=ts, wz=wz):
                            ins = None
                            for ic in range(4):
                                for fc in range(DCH):
                                    ins = t.matmul(
                                        PB[:, ic * 512:(ic + 1) * 512],
                                        wz["u"][:, fc * 512 + ic * 128:fc * 512 + ic * 128 + 128],
                                        A1[:, fc * XSP + 2 + ts * 512:fc * XSP + 2 + (ts + 1) * 512],
                                        start=(fc == 0), stop=(fc == 7))
                            return ins
                        mmu = et(umm, waits=((s_ev, s_ev.n),))
                        asg = ea(lambda a: a.activation(
                            A1[:, sg_base:sg_base + 2048], PA[:, 0:2048], AF.Silu),
                            waits=((s_mm, mmg), (s_ac, s_ac.n)))
                        hreg = A1[:, h_base + (ts % 2) * 2048:h_base + (ts % 2) * 2048 + 2048]
                        hv = ev(lambda v, hreg=hreg: v.tensor_tensor(
                            hreg, PB[:, 0:2048], A1[:, sg_base:sg_base + 2048],
                            ALU.mult), waits=((s_mm, mmu), (s_ac, asg),
                                              (s_ev, s_ev.n)))
                        for wave in range(2):
                            pbank = PA if wave == 0 else PB
                            wv_ = [(s_ev, hv)]
                            if wave == 0:
                                wv_.append((s_ac, asg))

                            def dmm(t, ts=ts, wz=wz, wave=wave, pbank=pbank,
                                    hreg=hreg):
                                ins = None
                                for j in range(2):
                                    ttl = wave * 2 + j
                                    for fo in range(2):
                                        for ic in range(4):
                                            ins = t.matmul(
                                                pbank[:, (j * 2 + fo) * 512:(j * 2 + fo + 1) * 512],
                                                hreg[:, ic * 512 + ttl * 128:ic * 512 + ttl * 128 + 128],
                                                wz["d"][:, ic * 1024 + fo * 512:ic * 1024 + (fo + 1) * 512],
                                                start=(ic == 0), stop=(ic == 3))
                                return ins
                            mmd = et(dmm, waits=tuple(wv_))
                            lastd = 0
                            for j in range(2):
                                ttl = wave * 2 + j
                                i = ts * 4 + ttl
                                tt = th * 8 + i
                                for fo in range(2):
                                    rt = resid[:, (b * NTT + tt) * D + fo * 512:
                                               (b * NTT + tt) * D + (fo + 1) * 512]
                                    lastd = ev(lambda v, rt=rt, pbank=pbank,
                                               j=j, fo=fo, i=i, e=e:
                                               v.scalar_tensor_tensor(
                                                   rt, pbank[:, (j * 2 + fo) * 512:(j * 2 + fo + 1) * 512],
                                                   comb[:, i * 16 + e:i * 16 + e + 1],
                                                   rt, ALU.mult, ALU.add),
                                               waits=((s_mm, mmd),
                                                      (s_ev, resid_ev[b * NTT + tt])))
                                    resid_ev[b * NTT + tt] = lastd
                    slot_mm[sl] = s_mm.n

                lcv = wld(A2[:, 0:12288], cvb,
                          waits=((s_mm, s_mm.n), (s_stg, stage_cv)), sem=s_w2)
                lsu = wld(A3[:, 0:8192], shsb, sem=s_w0,
                          waits=((s_mm, s_mm.n),))
                lsd = lsu
                for ts in range(2):
                    def cmm(t, ts=ts):
                        ins = None
                        for oc in range(4):
                            first = True
                            for k in range(3):
                                for fc in range(DCH):
                                    ins = t.matmul(
                                        PA[:, oc * 512:(oc + 1) * 512],
                                        A2[:, k * 4096 + fc * 512 + oc * 128:
                                           k * 4096 + fc * 512 + oc * 128 + 128],
                                        A1[:, fc * XSP + ts * 512 + k:fc * XSP + ts * 512 + k + 512],
                                        start=first, stop=(k == 2 and fc == 7))
                                    first = False
                        return ins
                    mmc = et(cmm, waits=((s_w2, lcv), (s_ev, s_ev.n),
                                         (s_ac, s_ac.n)))

                    def summ(t, ts=ts):
                        ins = None
                        for ic in range(4):
                            for fc in range(DCH):
                                ins = t.matmul(
                                    PB[:, ic * 512:(ic + 1) * 512],
                                    A3[:, fc * 512 + ic * 128:fc * 512 + ic * 128 + 128],
                                    A1[:, fc * XSP + 2 + ts * 512:fc * XSP + 2 + (ts + 1) * 512],
                                    start=(fc == 0), stop=(fc == 7))
                        return ins
                    mmup = et(summ, waits=((s_w0, lsu),))
                    asg = ea(lambda a: a.activation(
                        A1[:, sg_base:sg_base + 2048], PA[:, 0:2048], AF.Silu),
                        waits=((s_mm, mmc), (s_ac, s_ac.n)))
                    hreg = A1[:, h_base:h_base + 2048]
                    hv = ev(lambda v, hreg=hreg: v.tensor_tensor(
                        hreg, PB[:, 0:2048], A1[:, sg_base:sg_base + 2048],
                        ALU.mult), waits=((s_mm, mmup), (s_ac, asg),
                                          (s_ev, s_ev.n)))
                    for wave in range(2):
                        pbank = PA if wave == 0 else PB
                        wv_ = [(s_ev, hv), (s_w0, lsd)]
                        if wave == 0:
                            wv_.append((s_ac, asg))

                        def dmm(t, ts=ts, wave=wave, pbank=pbank, hreg=hreg):
                            ins = None
                            for j in range(2):
                                ttl = wave * 2 + j
                                for fo in range(2):
                                    for ic in range(4):
                                        ins = t.matmul(
                                            pbank[:, (j * 2 + fo) * 512:(j * 2 + fo + 1) * 512],
                                            hreg[:, ic * 512 + ttl * 128:ic * 512 + ttl * 128 + 128],
                                            A3[:, 4096 + ic * 1024 + fo * 512:
                                               4096 + ic * 1024 + (fo + 1) * 512],
                                            start=(ic == 0), stop=(ic == 3))
                            return ins
                        mmd = et(dmm, waits=tuple(wv_))
                        for j in range(2):
                            ttl = wave * 2 + j
                            i = ts * 4 + ttl
                            tt = th * 8 + i
                            for fo in range(2):
                                rt = resid[:, (b * NTT + tt) * D + fo * 512:
                                           (b * NTT + tt) * D + (fo + 1) * 512]
                                lastd = ev(lambda v, rt=rt, pbank=pbank, j=j,
                                           fo=fo: v.tensor_tensor(
                                               rt, rt,
                                               pbank[:, (j * 2 + fo) * 512:(j * 2 + fo + 1) * 512],
                                               ALU.add),
                                           waits=((s_mm, mmd),
                                                  (s_ev, resid_ev[b * NTT + tt])))
                                resid_ev[b * NTT + tt] = lastd
                slot_mm[0] = slot_mm[1] = s_mm.n

        if PHASES & 2:
            ffn_phase(0)

        b3_wr = [A3[:, 0:1024], A3[:, 1024:2048]]
        b3_xs = A3[:, 2048:4096]
        b3_Q = A3[:, 4096:6144]
        b3_tab = A3[:, 6144:8192]
        b3_E = [A3[:, 8192:8448], A3[:, 8448:8704], A3[:, 8704:8960]]
        b3_rt = A3[:, 8960:9984]
        b3_xtok = A3[:, 4096:5120]
        b3_tgA = A3[:, 4096:5120].bitcast(F32)
        b3_tgB = A3[:, 5120:6144].bitcast(F32)
        b3_tgI = A3[:, 8960:9984].bitcast(I32)

        if PHASES & 4:
            for b in range(B):
                phase_reset()
                kv_ev = s_ev.n
                for tsl in range(8):
                    xe = 0
                    for j in range(2):
                        tt = tsl * 2 + j
                        xe = ln_xhat(b, tt, tt, lambda c, j=j: b3_xs[
                            :, c * 256 + j * 128:c * 256 + (j + 1) * 128],
                            b3_xtok, 0, extra_w=((s_ev, kv_ev),))
                    xs_ready = xe
                    tb_ev = 0
                    for cc_ in range(4):
                        aa1, ee1 = gen_table(
                            b3_tab[:, cc_ * 256:(cc_ + 1) * 256],
                            ivf[:, 1 + cc_:2 + cc_], tsl * 256, 256, True,
                            b3_tgB, b3_tgB, b3_tgI) if False else (None, None)
                    for c in range(DCH):
                        lw = wld(b3_wr[c % 2], wkb[c],
                                 waits=((s_mm, s_mm.n), (s_stg, stage_attn_b)),
                                 sem=s_w0 if c % 2 == 0 else s_w1)

                        def kmm(t, c=c):
                            ins = None
                            for fc in range(DCH):
                                ins = t.matmul(
                                    PB[:, 1536:1792],
                                    b3_wr[c % 2][:, fc * 128:(fc + 1) * 128],
                                    b3_xs[:, fc * 256:(fc + 1) * 256],
                                    start=(fc == 0), stop=(fc == 7))
                            return ins
                        mmk = et(kmm, waits=((s_w0 if c % 2 == 0 else s_w1, lw),
                                             (s_ev, xs_ready), (s_ac, s_ac.n)))
                        ke = ea(lambda a, c=c, tsl=tsl: a.activation(
                            a2_K[:, c * 2048 + tsl * 256:c * 2048 + (tsl + 1) * 256]
                            if False else
                            A2[:, c * 2048 + tsl * 256:c * 2048 + (tsl + 1) * 256],
                            PB[:, 1536:1792], AF.Copy),
                            waits=((s_mm, mmk), (s_ac, s_ac.n)))
                    kraw = s_ac.n
                    tba = 0
                    for cc_ in range(4):
                        _, t1 = gen_table(b3_tab[:, cc_ * 256:(cc_ + 1) * 256],
                                          ivf[:, 1 + cc_:2 + cc_], tsl * 256, 256,
                                          True, b3_tgA, b3_tgB, b3_tgI,
                                          xw=((s_ac, kraw), (s_ev, s_ev.n)))
                        _, t2 = gen_table(b3_tab[:, 1024 + cc_ * 256:1024 + (cc_ + 1) * 256],
                                          ivf[:, 1 + cc_:2 + cc_], tsl * 256, 256,
                                          False, b3_tgA, b3_tgB, b3_tgI)
                        tba = max(t1, t2, s_ac.n, s_ev.n)
                    tabs_ev, tabs_ac = s_ev.n, s_ac.n
                    for cc_ in range(4):
                        ke_lo = A2[:, cc_ * 2048 + tsl * 256:cc_ * 2048 + (tsl + 1) * 256]
                        ke_hi = A2[:, (cc_ + 4) * 2048 + tsl * 256:(cc_ + 4) * 2048 + (tsl + 1) * 256]
                        Cc = b3_tab[:, cc_ * 256:(cc_ + 1) * 256]
                        Sc = b3_tab[:, 1024 + cc_ * 256:1024 + (cc_ + 1) * 256]
                        z0 = ev(lambda v, a=ke_lo, Cc=Cc: v.tensor_tensor(
                            b3_rt[:, 0:256], a, Cc, ALU.mult),
                            waits=((s_ac, tabs_ac), (s_ev, tabs_ev),
                                   (s_ev, s_ev.n)))
                        z1 = ev(lambda v, bq=ke_hi, Sc=Sc: v.tensor_tensor(
                            b3_rt[:, 256:512], bq, Sc, ALU.mult),
                            waits=((s_ev, z0),))
                        z2 = ev(lambda v, bq=ke_hi, Cc=Cc: v.tensor_tensor(
                            b3_rt[:, 512:768], bq, Cc, ALU.mult),
                            waits=((s_ev, z1),))
                        z3 = ev(lambda v, a=ke_lo, Sc=Sc: v.tensor_tensor(
                            b3_rt[:, 768:1024], a, Sc, ALU.mult),
                            waits=((s_ev, z2),))
                        z4 = ev(lambda v, a=ke_lo: v.tensor_tensor(
                            a, b3_rt[:, 0:256], b3_rt[:, 256:512], ALU.subtract),
                            waits=((s_ev, z3),))
                        kv_ev = ev(lambda v, bq=ke_hi: v.tensor_tensor(
                            bq, b3_rt[:, 512:768], b3_rt[:, 768:1024], ALU.add),
                            waits=((s_ev, z4),))
                    wrv = A3[:, 0:2048]
                    for dq in range(4):
                        lw = wld(wrv, wvb[dq],
                                 waits=((s_mm, s_mm.n), (s_stg, stage_attn_b)),
                                 sem=s_w0)
                        for j in range(2):
                            def vmm(t, dq=dq, j=j, wrv=wrv):
                                ins = None
                                for fc in range(DCH):
                                    ins = t.matmul(
                                        PB[:, 1536:1792],
                                        b3_xs[:, fc * 256 + j * 128:fc * 256 + (j + 1) * 128],
                                        wrv[:, fc * 256:(fc + 1) * 256],
                                        start=(fc == 0), stop=(fc == 7))
                                return ins
                            mmv = et(vmm, waits=((s_w0, lw),
                                                 (s_ev, xs_ready),
                                                 (s_ev, kv_ev), (s_ac, s_ac.n)))
                            tt = tsl * 2 + j
                            kv_ev = ev(lambda v, tt=tt, dq=dq: v.tensor_copy(
                                A1[:, tt * 1024 + dq * 256:tt * 1024 + (dq + 1) * 256],
                                PB[:, 1536:1792]), waits=((s_mm, mmv),))
                kvb_ready = kv_ev

                ev_prev = kvb_ready
                for qsb in range(8):
                    if qsb > QSLIM:
                        continue
                    qt0, qt1 = qsb * 2, qsb * 2 + 1
                    for cc_ in range(4):
                        gen_table(b3_tab[:, cc_ * 256:(cc_ + 1) * 256],
                                  ivf[:, 1 + cc_:2 + cc_], qsb * 256, 256, True,
                                  b3_tgA, b3_tgB, b3_tgI,
                                  xw=((s_ev, ev_prev), (s_mm, s_mm.n)))
                        gen_table(b3_tab[:, 1024 + cc_ * 256:1024 + (cc_ + 1) * 256],
                                  ivf[:, 1 + cc_:2 + cc_], qsb * 256, 256, False,
                                  b3_tgA, b3_tgB, b3_tgI)
                    tabs_ev, tabs_ac = s_ev.n, s_ac.n
                    xq_ev = 0
                    for j in range(2):
                        tt = qsb * 2 + j
                        sc = tt
                        rt = resid[:, (b * NTT + tt) * D:(b * NTT + tt + 1) * D]
                        rcol = stats[:, 2 * sc + 1:2 * sc + 2]
                        mcol = stats[:, 2 * sc:2 * sc + 1]
                        v3 = ev(lambda v, m=mcol, rc=rcol: v.tensor_scalar(
                            small[:, 23:24], m, rc, -1.0, ALU.mult, ALU.mult),
                            waits=((s_ev, tabs_ev), (s_ev, s_ev.n)))
                        a2_ = ea(lambda a, rt=rt, rc=rcol: a.activation(
                            b3_xtok, rt, AF.Identity, bias=small[:, 23:24], scale=rc),
                            waits=((s_ev, v3), (s_ac, tabs_ac), (s_ac, s_ac.n),
                                   (s_mm, ln_xt[0])))

                        def tfn(t, j=j):
                            ins = None
                            for c in range(DCH):
                                ins = t.transpose(PT[0][:, c * 128:(c + 1) * 128],
                                                  b3_xtok[:, c * 128:(c + 1) * 128],
                                                  ident[:])
                            return ins
                        tr = et(tfn, waits=((s_ac, a2_), (s_ev, ln_pt[0])))
                        ln_xt[0] = tr
                        last = 0
                        for c in range(DCH):
                            last = ev(lambda v, c=c, j=j: v.tensor_copy(
                                b3_xs[:, c * 256 + j * 128:c * 256 + (j + 1) * 128],
                                PT[0][:, c * 128:(c + 1) * 128]),
                                waits=((s_mm, tr),))
                        ln_pt[0] = last
                        xq_ev = last
                    qro_ev = 0
                    for c in range(DCH):
                        lw = wld(b3_wr[c % 2], wqb[c],
                                 waits=((s_mm, s_mm.n), (s_stg, stage_attn_b)),
                                 sem=s_w0 if c % 2 == 0 else s_w1)

                        def qmm(t, c=c):
                            ins = None
                            for fc in range(DCH):
                                ins = t.matmul(
                                    PB[:, 1536:1792],
                                    b3_wr[c % 2][:, fc * 128:(fc + 1) * 128],
                                    b3_xs[:, fc * 256:(fc + 1) * 256],
                                    start=(fc == 0), stop=(fc == 7))
                            return ins
                        mmq = et(qmm, waits=((s_w0 if c % 2 == 0 else s_w1, lw),
                                             (s_ev, xq_ev), (s_ac, s_ac.n)))
                        qro_ev = ea(lambda a, c=c: a.activation(
                            b3_Q[:, c * 256:(c + 1) * 256], PB[:, 1536:1792],
                            AF.Copy), waits=((s_mm, mmq), (s_ac, s_ac.n)))
                    qraw = qro_ev
                    rope_ev = 0
                    for cc_ in range(4):
                        qlo = b3_Q[:, cc_ * 256:(cc_ + 1) * 256]
                        qhi = b3_Q[:, (cc_ + 4) * 256:(cc_ + 5) * 256]
                        Cc = b3_tab[:, cc_ * 256:(cc_ + 1) * 256]
                        Sc = b3_tab[:, 1024 + cc_ * 256:1024 + (cc_ + 1) * 256]
                        z0 = ev(lambda v, a=qlo, Cc=Cc: v.tensor_tensor(
                            b3_rt[:, 0:256], a, Cc, ALU.mult),
                            waits=((s_ac, qraw), (s_ev, s_ev.n)))
                        z1 = ev(lambda v, bq=qhi, Sc=Sc: v.tensor_tensor(
                            b3_rt[:, 256:512], bq, Sc, ALU.mult),
                            waits=((s_ev, z0),))
                        z2 = ev(lambda v, bq=qhi, Cc=Cc: v.tensor_tensor(
                            b3_rt[:, 512:768], bq, Cc, ALU.mult),
                            waits=((s_ev, z1),))
                        z3 = ev(lambda v, a=qlo, Sc=Sc: v.tensor_tensor(
                            b3_rt[:, 768:1024], a, Sc, ALU.mult),
                            waits=((s_ev, z2),))
                        z4 = ev(lambda v, a=qlo: v.tensor_tensor(
                            a, b3_rt[:, 0:256], b3_rt[:, 256:512], ALU.subtract),
                            waits=((s_ev, z3),))
                        rope_ev = ev(lambda v, bq=qhi: v.tensor_tensor(
                            bq, b3_rt[:, 512:768], b3_rt[:, 768:1024], ALU.add),
                            waits=((s_ev, z4),))
                    qrdy = rope_ev

                    mx_ev = 0
                    for j in range(2):
                        qt = qt0 + j
                        nk512 = (qt * 128) // 512
                        od = (qt * 128) % 512
                        mcol = small[:, 24 + j:25 + j]
                        first = True
                        for kc5 in range(nk512 + 1):
                            pmx = PA[:, (kc5 % 2) * 512:(kc5 % 2) * 512 + 512]

                            def mxm(t, j=j, kc5=kc5, pmx=pmx):
                                ins = None
                                for fc in range(DCH):
                                    ins = t.matmul(
                                        pmx,
                                        b3_Q[:, fc * 256 + j * 128:fc * 256 + (j + 1) * 128],
                                        A2[:, fc * 2048 + kc5 * 512:fc * 2048 + (kc5 + 1) * 512],
                                        start=(fc == 0), stop=(fc == 7))
                                return ins
                            mmx = et(mxm, waits=((s_ev, qrdy), (s_ev, mx_ev)))
                            if kc5 < nk512:
                                mx_ev = ev(lambda v, pmx=pmx: v.reduce_max(
                                    small[:, 26:27], pmx, axis=AX.X),
                                    waits=((s_mm, mmx), (s_ev, s_ev.n)))
                            else:
                                if od > 0:
                                    mx_ev = ev(lambda v, od=od, pmx=pmx: v.reduce_max(
                                        small[:, 26:27], pmx[:, 0:od], axis=AX.X),
                                        waits=((s_mm, mmx), (s_ev, s_ev.n)))
                                else:
                                    mx_ev = ev(lambda v: v.memset(
                                        small[:, 26:27], 0.0),
                                        waits=((s_mm, mmx), (s_ev, s_ev.n)))
                                mx_ev = ev(lambda v, od=od, pmx=pmx: v.tensor_tensor(
                                    b3_rt[:, 0:128], pmx[:, od:od + 128], tri[:],
                                    ALU.mult), waits=((s_ev, mx_ev),))
                                mx_ev = ev(lambda v, od=od, pmx=pmx: v.tensor_tensor(
                                    b3_rt[:, 128:256], pmx[:, od:od + 128],
                                    b3_rt[:, 0:128], ALU.subtract),
                                    waits=((s_ev, mx_ev),))
                                mx_ev = ev(lambda v, od=od, pmx=pmx: v.tensor_tensor(
                                    b3_rt[:, 0:128], pmx[:, od:od + 128], ident[:],
                                    ALU.mult), waits=((s_ev, mx_ev),))
                                mx_ev = ev(lambda v: v.tensor_tensor(
                                    b3_rt[:, 128:256], b3_rt[:, 128:256],
                                    b3_rt[:, 0:128], ALU.add),
                                    waits=((s_ev, mx_ev),))
                                mx_ev = ev(lambda v: v.reduce_max(
                                    small[:, 27:28], b3_rt[:, 128:256], axis=AX.X),
                                    waits=((s_ev, mx_ev),))
                                mx_ev = ev(lambda v: v.tensor_tensor(
                                    small[:, 26:27], small[:, 26:27],
                                    small[:, 27:28], ALU.max),
                                    waits=((s_ev, mx_ev),))
                            if first:
                                mx_ev = ev(lambda v, m=mcol: v.tensor_copy(
                                    m, small[:, 26:27]), waits=((s_ev, mx_ev),))
                                first = False
                            else:
                                mx_ev = ev(lambda v, m=mcol: v.tensor_tensor(
                                    m, m, small[:, 26:27], ALU.max),
                                    waits=((s_ev, mx_ev),))
                        mx_ev = ev(lambda v, m=mcol: v.tensor_scalar(
                            m, m, 0.0, None, ALU.max), waits=((s_ev, mx_ev),))
                    shrb = b3_rt[0:1, 768:1024]
                    cbx = ev(lambda v: v.tensor_copy(b3_rt[:, 256:258],
                             small[:, 24:26]), waits=((s_ev, mx_ev),))

                    def trx(t):
                        t.transpose(PT[0][0:1, 0:128], b3_rt[:, 256:257], ident[:])
                        return t.transpose(PT[0][0:1, 128:256], b3_rt[:, 257:258],
                                           ident[:])
                    trm2 = et(trx, waits=((s_ev, cbx), (s_ev, ln_pt[0])))
                    shx = ev(lambda v: v.tensor_scalar(shrb, PT[0][0:1, 0:256],
                             -1.0, None, ALU.mult),
                             waits=((s_mm, trm2), (s_ev, s_ev.n)))
                    ln_pt[0] = shx
                    shift_rdy = shx

                    spq_guard = [0, 0]
                    ering = [0, 0, 0]
                    av_cnt = [0, 0, 0]
                    nkc = qsb * 2 + 2
                    for kc in range(nkc):
                        spq = PA[:, (kc % 2) * 512:(kc % 2) * 512 + 256]

                        def smm(t, kc=kc, spq=spq):
                            for fc in range(DCH):
                                t.matmul(
                                    spq,
                                    A2[:, fc * 2048 + kc * 128:fc * 2048 + (kc + 1) * 128],
                                    b3_Q[:, fc * 256:(fc + 1) * 256],
                                    start=(fc == 0), stop=False,
                                    skip_group_check=True)
                            return t.matmul(spq, e0m[:, :],
                                            b3_rt[0:32, 768:1024],
                                            start=False, stop=True,
                                            skip_group_check=True)
                        w = [(s_ev, qrdy), (s_ev, shift_rdy), (s_mm, trm2)]
                        if kc >= 2:
                            w.append((s_ac, spq_guard[kc % 2]))
                        mmS = et(smm, waits=tuple(w))
                        slot = kc % 3
                        wv_ = [(s_mm, mmS), (s_ac, s_ac.n)]
                        if kc >= 3:
                            wv_.append((s_mm, av_cnt[slot]))
                        ex = ea(lambda a, spq=spq, slot=slot: a.activation(
                            b3_E[slot], spq, AF.Exp, scale=1.0 / 32),
                            waits=tuple(wv_))
                        spq_guard[kc % 2] = ex
                        if kc == qt0:
                            ex2 = ev(lambda v, slot=slot: v.tensor_tensor(
                                b3_E[slot][:, 0:128], b3_E[slot][:, 0:128],
                                tri[:], ALU.mult), waits=((s_ac, ex), (s_ev, s_ev.n)))
                        elif kc == qt1:
                            ex2 = ev(lambda v, slot=slot: v.tensor_tensor(
                                b3_E[slot][:, 128:256], b3_E[slot][:, 128:256],
                                tri[:], ALU.mult), waits=((s_ac, ex), (s_ev, s_ev.n)))
                            ex2 = ev(lambda v, slot=slot: v.memset(
                                b3_E[slot][:, 0:128], 0.0), waits=((s_ev, ex2),))
                        else:
                            ex2 = ev(lambda v: v.tensor_copy(
                                small[:, 30:31], small[:, 30:31]),
                                waits=((s_ac, ex), (s_ev, s_ev.n)))
                        ering[slot] = ex2

                        def avmm(t, kc=kc, slot=slot, qt0=qt0):
                            ed = b3_E[slot]
                            ins = None
                            for j in range(2):
                                qt = qt0 + j
                                if kc > qt:
                                    continue
                                pav = (PA if j == 0 else PB)
                                base = 1024 if j == 0 else 0
                                for dh in range(2):
                                    ins = t.matmul(
                                        pav[:, base + dh * 512:base + (dh + 1) * 512],
                                        ed[:, j * 128:(j + 1) * 128],
                                        A1[:, kc * 1024 + dh * 512:kc * 1024 + (dh + 1) * 512],
                                        start=(kc == 0), stop=(kc == qt),
                                        skip_group_check=True)
                                ins = t.matmul(
                                    PB[:, 1024 + 512 * j:1025 + 512 * j],
                                    ed[:, j * 128:(j + 1) * 128], onescol[:],
                                    start=(kc == 0), stop=(kc == qt),
                                    skip_group_check=True)
                            return ins
                        wa_ = [(s_ev, ering[slot])]
                        if kc == 0:
                            wa_.append((s_ev, ev_prev))
                        av_cnt[slot] = et(avmm, waits=tuple(wa_))
                    av_done = av_cnt[(nkc - 1) % 3]

                    rcq = ev(lambda v: v.reciprocal(small[:, 28:29],
                             PB[:, 1024:1025]), waits=((s_mm, av_done),
                                                       (s_ev, s_ev.n)))
                    rcq = ev(lambda v: v.reciprocal(small[:, 29:30],
                             PB[:, 1536:1537]), waits=((s_ev, rcq),))
                    eout = rcq
                    for j in range(2):
                        qt = qt0 + j
                        tt = qt
                        pav = (PA if j == 0 else PB)
                        base = 1024 if j == 0 else 0
                        for dh in range(2):
                            rt = resid[:, (b * NTT + tt) * D + dh * 512:
                                       (b * NTT + tt) * D + (dh + 1) * 512]
                            eout = ev(lambda v, rt=rt, pav=pav, base=base,
                                      dh=dh, j=j: v.scalar_tensor_tensor(
                                          rt, pav[:, base + dh * 512:base + (dh + 1) * 512],
                                          small[:, 28 + j:29 + j], rt,
                                          ALU.mult, ALU.add),
                                      waits=((s_ev, rcq),
                                             (s_ev, resid_ev[b * NTT + tt])))
                            resid_ev[b * NTT + tt] = eout
                    ev_prev = eout
                if QSLIM < 99:
                    pq_ev = ev(lambda v: v.tensor_copy(
                        small[:, 40:41], PB[:, 1024:1025]),
                        waits=((s_ev, s_ev.n), (s_mm, s_mm.n), (s_ac, s_ac.n)))
                    pq_ev = ev(lambda v: v.tensor_copy(
                        small[:, 41:42], PB[:, 1536:1537]),
                        waits=((s_ev, pq_ev),))
                    pq_ev = ev(lambda v: v.tensor_copy(
                        small[:, 42:46], PA[:, 1024:1028]),
                        waits=((s_ev, pq_ev),))
                    pq_ev = ev(lambda v: v.tensor_copy(
                        small[:, 46:48], PA[:, 1536:1538]),
                        waits=((s_ev, pq_ev),))

        if PHASES & 8:
            ffn_phase(1)

        wfin = ((s_ev, s_ev.n), (s_ac, s_ac.n), (s_mm, s_mm.n),
                (s_gp, s_gp.n), (s_ld, s_ld.n), (s_w0, s_w0.n),
                (s_w1, s_w1.n), (s_w2, s_w2.n))
        st = 0
        for b in range(B):
            st = xdma(y_out[b].rearrange("(tt p) d -> p tt d", p=128),
                      resid[:, b * NTT * D:(b + 1) * NTT * D]
                      .rearrange("p (tt d) -> p tt d", tt=NTT),
                      waits=wfin if b == 0 else ())
        if DUMP:
            for nm, t_ in (("A1", A1), ("A2", A2), ("A3", A3), ("small", small),
                           ("bcsb", bcsb), ("stats", stats[:, 0:32])):
                st = xdma(dumps[nm], t_[:, :] if nm != "stats" else t_, waits=wfin)
        prog.emit("sync", lambda g: g.nop(), waits=((s_xd, st),))

        with nc.allow_low_precision("bf16 operands with fp32 accumulation by design"):
            with nc.Block() as block:
                prog.lower(block)

    return nc


def _prep(inputs):
    f = np.float32
    qkv_w = np.asarray(inputs["L0_attn_a_qkv_w"], f)
    qkv_b = np.asarray(inputs["L0_attn_a_qkv_b"], f)
    out_w = np.asarray(inputs["L0_attn_a_out_w"], f)
    out_b = np.asarray(inputs["L0_attn_a_out_b"], f)
    wq = np.asarray(inputs["L0_attn_b_wq"], f)
    wk = np.asarray(inputs["L0_attn_b_wk"], f)
    wv = np.asarray(inputs["L0_attn_b_wv"], f)

    pa = np.concatenate([np.concatenate([64 * h + np.arange(0, 64, 2),
                                         64 * h + np.arange(1, 64, 2)])
                         for h in range(NH)])
    pb = np.concatenate([np.arange(0, D, 2), np.arange(1, D, 2)])

    qkvTm = qkv_w.T.copy()
    qkvTm[:, 0:D] = qkvTm[:, 0:D][:, pa]
    qkvTm[:, D:2 * D] = qkvTm[:, D:2 * D][:, pa]
    qkv_bm = qkv_b.copy()
    qkv_bm[0:D] = qkv_b[0:D][pa]
    qkv_bm[D:2 * D] = qkv_b[D:2 * D][pa]

    def pcf(w):
        C = w.shape[0] // 128
        return np.ascontiguousarray(
            w.reshape(C, 128, -1).transpose(1, 0, 2).reshape(128, -1))

    qkvTparts = np.stack([
        np.stack([pcf(qkvTm[:, w * D + g * 256:w * D + (g + 1) * 256])
                  for g in range(NG)])
        for w in range(3)])
    outw = np.stack([pcf_o(out_w, g) for g in range(NG)])
    kvp = np.ascontiguousarray(np.concatenate([qkvTparts[1], qkvTparts[2]], axis=2))
    qop = np.ascontiguousarray(np.concatenate([qkvTparts[0], outw], axis=2))

    f32 = np.float32
    eg = np.asarray(inputs["L0_ffn_expert_gate_w"], f)
    eu = np.asarray(inputs["L0_ffn_expert_up_w"], f)
    edw = np.asarray(inputs["L0_ffn_expert_down_w"], f)
    ew = np.ascontiguousarray(np.stack([
        np.concatenate([pcf(eg[e]), pcf(eu[e]), pcf(edw[e])], axis=1)
        for e in range(NE)]))
    cvt = np.asarray(inputs["L0_ffn_conv_w"], f).transpose(2, 1, 0)
    cv = np.ascontiguousarray(np.concatenate([pcf(cvt[k]) for k in range(3)], axis=1))
    shs = np.ascontiguousarray(np.concatenate(
        [pcf(np.asarray(inputs["L0_ffn_shared_up_w"], f)),
         pcf(np.asarray(inputs["L0_ffn_shared_down_w"], f))], axis=1))

    return {
        "kvp": kvp,
        "qop": qop,
        "qkv_b": np.ascontiguousarray(qkv_bm.reshape(24, 128).T),
        "out_b": np.ascontiguousarray(out_b),
        "wqT": np.stack([pcf(wq.T[:, pb][:, c * 128:(c + 1) * 128]) for c in range(8)]),
        "wkT": np.stack([pcf(wk.T[:, pb][:, c * 128:(c + 1) * 128]) for c in range(8)]),
        "wvT": np.stack([pcf(wv.T[:, dq * 256:(dq + 1) * 256]) for dq in range(4)]),
        "router_w": pcf(np.asarray(inputs["L0_ffn_router_w"], f)),
        "router_b": np.ascontiguousarray(np.asarray(inputs["L0_ffn_router_bias"], f)),
        "ew": ew,
        "convT": cv,
        "shs": shs,
    }


def pcf_o(out_w, g):
    sl = out_w.T[g * 256:(g + 1) * 256, :]
    return np.ascontiguousarray(sl.reshape(2, 128, 1024).transpose(1, 0, 2).reshape(128, -1))


def kernel(**inputs):
    x = np.ascontiguousarray(np.asarray(inputs["L0_attn_a_x"], np.float32))
    shared = _prep(inputs)
    nc = build()
    in_maps = []
    for c in range(8):
        m = dict(shared)
        m["x"] = np.ascontiguousarray(x[2 * c:2 * c + 2])
        in_maps.append(m)
    res = run_bass_kernel_spmd(nc, in_maps, core_ids=list(range(8)))
    out = np.empty((16, S, D), np.float32)
    for c in range(8):
        out[2 * c:2 * c + 2] = res.results[c]["y"]
    return out
